# version 11 (final)
# baseline (speedup 1.0000x reference)
import numpy as np
import concourse.bass as bass
import concourse.mybir as mybir
from concourse.bass_utils import run_bass_kernel_spmd
from contextlib import ExitStack

F32 = mybir.dt.float32
BF16 = mybir.dt.bfloat16
AF = mybir.ActivationFunctionType
ALU = mybir.AluOpType
AX = mybir.AxisListType

T = 4096
D = 1024
DF = 4096
NT = 32
NTC = 16
EPS = 1e-5
QS = 1.0 / 32.0
ENGS = ["sync", "vector", "tensor", "scalar", "gpsimd"]


class MockInst:
    def then_inc(self, sem, k=16):
        return self


class MockEng:
    def __getattr__(self, name):
        def f(*a, **kw):
            return MockInst()
        return f


class Prod:

    def __init__(self, sem, record, toks):
        self.sem = sem
        self.n = 0
        self.record = record
        self.toks = toks

    def inc(self, inst, key=None, k=1):
        self.n += k
        if self.record:
            if key is not None:
                self.toks[key] = self.n
        else:
            inst.then_inc(self.sem, k)
        return self.n


def build_and_emit():
    nc = bass.Bass("TRN2", target_bir_lowering=False)

    x_hbm = nc.dram_tensor("x", [T, D], F32, kind="ExternalInput").ap()
    wq_h = nc.dram_tensor("wq", [D, D], F32, kind="ExternalInput").ap()
    wk_h = nc.dram_tensor("wk", [D, D], F32, kind="ExternalInput").ap()
    wv_h = nc.dram_tensor("wv", [D, D], F32, kind="ExternalInput").ap()
    wiT_h = nc.dram_tensor("wiT", [D, DF], F32, kind="ExternalInput").ap()
    woT_h = nc.dram_tensor("woT", [DF, D], F32, kind="ExternalInput").ap()
    biP_h = nc.dram_tensor("biP", [128, 32], F32, kind="ExternalInput").ap()
    bo_h = nc.dram_tensor("bo", [1, D], F32, kind="ExternalInput").ap()
    aT_h = nc.dram_tensor("aT", [D, 128], F32, kind="ExternalInput").ap()
    bT_h = nc.dram_tensor("bT", [128, DF], F32, kind="ExternalInput").ap()
    gwT_h = nc.dram_tensor("gwT", [D, 8], F32, kind="ExternalInput").ap()
    gb_h = nc.dram_tensor("gb", [1, 8], F32, kind="ExternalInput").ap()
    idf_h = nc.dram_tensor("idf", [128, 128], F32, kind="ExternalInput").ap()
    io8_h = nc.dram_tensor("io8", [128, 8], F32, kind="ExternalInput").ap()
    ioPB_h = nc.dram_tensor("ioPB", [128, 8], F32, kind="ExternalInput").ap()
    ediv_h = nc.dram_tensor("ediv", [128, 1], F32, kind="ExternalInput").ap()
    out_hbm = nc.dram_tensor("out", [T, D], F32, kind="ExternalOutput").ap()
    x2s = nc.dram_tensor("x2s", [T, D], BF16, kind="Internal").ap()

    toks = {}
    mock = MockEng()

    with ExitStack() as ectx:
        e = ectx.enter_context
        U1 = e(nc.sbuf_tensor("U1", [128, 8 * T], BF16))
        U2 = e(nc.sbuf_tensor("U2", [128, NT * D], BF16))
        stats = e(nc.sbuf_tensor("stats", [128, 2 * NT], F32))
        idb = e(nc.sbuf_tensor("idb", [128, 128], BF16))
        idf = e(nc.sbuf_tensor("idf32", [128, 128], F32))
        seps = e(nc.sbuf_tensor("seps", [128, 1], F32))

        aw = e(nc.semaphore("aw")); av = e(nc.semaphore("av"))
        as_ = e(nc.semaphore("as")); at = e(nc.semaphore("at"))
        bw = e(nc.semaphore("bw")); bv = e(nc.semaphore("bv"))
        bs = e(nc.semaphore("bs")); bt = e(nc.semaphore("bt"))
        bg = e(nc.semaphore("bg"))
        cw = e(nc.semaphore("cw")); cv = e(nc.semaphore("cv"))
        cs = e(nc.semaphore("cs")); ct = e(nc.semaphore("ct"))
        cg = e(nc.semaphore("cg"))
        cx = e(nc.semaphore("cx"))
        bar = e(nc.semaphore("bar"))
        SEM = {"aw": aw, "av": av, "as": as_, "at": at,
               "bw": bw, "bv": bv, "bs": bs, "bt": bt, "bg": bg,
               "cw": cw, "cv": cv, "cs": cs, "ct": ct, "cg": cg, "cx": cx}

        def run_phase(emit):
            for en in ENGS:
                emit(en, mock, True)
            with nc.Block() as blk:
                blk.sync(lambda g: emit("sync", g, False))
                blk.vector(lambda g: emit("vector", g, False))
                blk.tensor(lambda g: emit("tensor", g, False))
                blk.scalar(lambda g: emit("scalar", g, False))
                blk.gpsimd(lambda g: emit("gpsimd", g, False))

        abctx = ExitStack()
        wk_sb = abctx.enter_context(nc.sbuf_tensor("wk_sb", [128, 8 * D], BF16))

        with ExitStack() as actx:
            ea = actx.enter_context
            wq_sb = ea(nc.sbuf_tensor("wq_sb", [128, 8 * D], BF16))
            wv_sb = ea(nc.sbuf_tensor("wv_sb", [128, 8 * D], BF16))
            xst = ea(nc.sbuf_tensor("xst", [128, 2 * D], F32))
            wst = xst
            ln1 = ea(nc.sbuf_tensor("ln1", [128, D], BF16))
            lnT = ea(nc.sbuf_tensor("lnT", [128, 8 * 512], BF16))
            bn6 = ea(nc.sbuf_tensor("bn6", [128, 12], F32))
            mv = ea(nc.sbuf_tensor("mv", [128, 2], F32))
            lnv = ea(nc.sbuf_tensor("lnv", [128, 1], F32))
            rstd = ea(nc.sbuf_tensor("rstd", [128, 1], F32))
            pT = ea(nc.psum_tensor("pT", [128, 1024], BF16))
            pQ = ea(nc.psum_tensor("pQ", [128, 1024], F32))
            pV = ea(nc.psum_tensor("pV", [128, 1024], F32))

            wl = []
            for c in range(8):
                wl.append((wq_sb, c * D, wq_h[c * 128:(c + 1) * 128, :], D))
            for c in range(8):
                wl.append((wk_sb, c * D, wk_h[c * 128:(c + 1) * 128, :], D))
            for c in range(8):
                wl.append((wv_sb, c * D, wv_h[c * 128:(c + 1) * 128, :], D))

            def emit_A(en, g, rec):
                def W(sm, key):
                    if not rec:
                        g.wait_ge(SEM[sm], toks[key])

                def WV(sm, val):
                    if not rec:
                        g.wait_ge(SEM[sm], val)

                if en == "sync":
                    w = Prod(aw, rec, toks)
                    w.inc(g.dma_start(out=idf[:], in_=idf_h), ("Aidf",), 16)
                    for k, (dst, col, src, ncol) in enumerate(wl):
                        if k >= 2:
                            W("av", ("Awc", k - 2))
                        w.inc(g.dma_start(
                            out=wst[:, (k % 2) * D:(k % 2) * D + ncol], in_=src),
                            ("Awd", k), 16)
                    for ti in range(NT):
                        if ti >= 2:
                            W("av", ("An", ti - 2))
                        else:
                            W("av", ("Awc", 22 + ti))
                        w.inc(g.dma_start(
                            out=xst[:, (ti % 2) * D:(ti % 2 + 1) * D],
                            in_=x_hbm[ti * 128:(ti + 1) * 128, :]),
                            ("Ax", ti), 16)
                    if not rec:
                        g.sem_inc(bar, 1)
                    return

                if en == "vector":
                    v = Prod(av, rec, toks)
                    g.memset(seps[:], EPS)
                    for k, (dst, col, src, ncol) in enumerate(wl):
                        WV("aw", 16 * (1 + k + 1))
                        v.inc(g.tensor_copy(
                            out=dst[:, col:col + ncol],
                            in_=wst[:, (k % 2) * D:(k % 2) * D + ncol]),
                            ("Awc", k))
                    W("aw", ("Aidf",))
                    v.inc(g.tensor_copy(out=idb[:], in_=idf[:]), ("Aid",))
                    for ti in range(NT):
                        W("aw", ("Ax", ti))
                        xs = xst[:, (ti % 2) * D:(ti % 2 + 1) * D]
                        g.bn_stats(bn6[:, 0:6], xs[:, 0:512])
                        i0 = v.inc(g.bn_stats(bn6[:, 6:12], xs[:, 512:1024]), None)
                        WV("av", i0)
                        i0 = v.inc(g.bn_aggr(mv[:], bn6[:]), ("Aag", ti))
                        WV("av", i0)
                        v.inc(g.tensor_copy(out=stats[:, 2 * ti:2 * ti + 1],
                                            in_=mv[:, 0:1]), None)
                        W("as", ("Ars", ti))
                        v.inc(g.tensor_copy(out=stats[:, 2 * ti + 1:2 * ti + 2],
                                            in_=rstd[:]), None)
                        v.inc(g.tensor_scalar(
                            out=ln1[:], in0=xs, scalar1=mv[:, 0:1],
                            scalar2=rstd[:], op0=ALU.subtract, op1=ALU.mult),
                            ("An", ti))
                        W("at", ("Atp", ti))
                        sloc = (ti % 4) * 128
                        ins = None
                        for c in range(8):
                            ins = g.tensor_copy(
                                out=lnT[:, c * 512 + sloc:c * 512 + sloc + 128],
                                in_=pT[:, c * 128:(c + 1) * 128])
                        v.inc(ins, ("AlnT", ti))
                        if ti % 4 == 3:
                            mb = ti // 4
                            for gg in range(8):
                                W("at", ("Atv", mb, gg))
                                vs8, n = divmod(gg, 2)
                                v.inc(g.tensor_copy(
                                    out=U2[:, (mb * 4 + vs8) * D + n * 512:
                                           (mb * 4 + vs8) * D + (n + 1) * 512],
                                    in_=pV[:, (gg % 2) * 512:(gg % 2 + 1) * 512]),
                                    ("Avev", mb, gg))
                    if not rec:
                        g.sem_inc(bar, 1)
                    return

                if en == "tensor":
                    t = Prod(at, rec, toks)
                    for ti in range(NT):
                        mb, sl = divmod(ti, 4)
                        W("av", ("An", ti))
                        if ti > 0:
                            W("av", ("AlnT", ti - 1))
                        else:
                            W("av", ("Aid",))
                        ins = None
                        for c in range(8):
                            ins = g.transpose(
                                pT[:, c * 128:(c + 1) * 128],
                                ln1[:, c * 128:(c + 1) * 128], idb[:])
                        t.inc(ins, ("Atp", ti))
                        if sl == 3:
                            W("av", ("AlnT", ti))
                            for cq in range(8):
                                if cq >= 2:
                                    W("as", ("Aqe", mb, cq - 2))
                                elif mb > 0:
                                    W("as", ("Aqe", mb - 1, cq + 6))
                                else:
                                    W("av", ("Awc", 7))
                                ins = None
                                for c in range(8):
                                    ins = g.matmul(
                                        pQ[:, (cq % 2) * 512:(cq % 2 + 1) * 512],
                                        wq_sb[:, c * D + cq * 128:
                                              c * D + (cq + 1) * 128],
                                        lnT[:, c * 512:(c + 1) * 512],
                                        start=(c == 0), stop=(c == 7))
                                t.inc(ins, ("Atq", mb, cq))
                            for gg in range(8):
                                vs8, n = divmod(gg, 2)
                                if gg >= 2:
                                    W("av", ("Avev", mb, gg - 2))
                                elif mb > 0:
                                    W("av", ("Avev", mb - 1, gg + 6))
                                else:
                                    W("av", ("Awc", 23))
                                ins = None
                                for c in range(8):
                                    ins = g.matmul(
                                        pV[:, (gg % 2) * 512:(gg % 2 + 1) * 512],
                                        lnT[:, c * 512 + vs8 * 128:
                                            c * 512 + (vs8 + 1) * 128],
                                        wv_sb[:, c * D + n * 512:
                                              c * D + (n + 1) * 512],
                                        start=(c == 0), stop=(c == 7))
                                t.inc(ins, ("Atv", mb, gg))
                    if not rec:
                        g.sem_inc(bar, 1)
                    return

                if en == "scalar":
                    s = Prod(as_, rec, toks)
                    for ti in range(NT):
                        mb, sl = divmod(ti, 4)
                        W("av", ("Aag", ti))
                        i0 = s.inc(g.activation(out=lnv[:], in_=mv[:, 1:2],
                                                func=AF.Ln, bias=seps[:]), None)
                        WV("as", i0)
                        s.inc(g.activation(out=rstd[:], in_=lnv[:], func=AF.Exp,
                                           scale=-0.5), ("Ars", ti))
                        if sl == 3:
                            for cq in range(8):
                                W("at", ("Atq", mb, cq))
                                s.inc(g.activation(
                                    out=U1[:, cq * T + mb * 512:
                                           cq * T + (mb + 1) * 512],
                                    in_=pQ[:, (cq % 2) * 512:(cq % 2 + 1) * 512],
                                    func=AF.Copy, scale=QS), ("Aqe", mb, cq))
                    if not rec:
                        g.sem_inc(bar, 1)
                    return

                if en == "gpsimd":
                    if not rec:
                        g.sem_inc(bar, 1)
                    return

            run_phase(emit_A)

        with ExitStack() as bctx:
            eb = bctx.enter_context
            xj = eb(nc.sbuf_tensor("xj", [128, 2 * D], F32))
            ln1j = eb(nc.sbuf_tensor("ln1j", [128, D], BF16))
            lnTj = eb(nc.sbuf_tensor("lnTj", [128, D], BF16))
            kTj = eb(nc.sbuf_tensor("kTj", [128, D], BF16))
            S_sb = eb(nc.sbuf_tensor("S_sb", [128, T], F32))
            P_bf = eb(nc.sbuf_tensor("P_bf", [128, T], BF16))
            PT_sb = eb(nc.sbuf_tensor("PT_sb", [128, T], BF16))
            attn = eb(nc.sbuf_tensor("attn", [128, D], F32))
            tres = eb(nc.sbuf_tensor("tres", [128, D], F32))
            x2j = eb(nc.sbuf_tensor("x2j", [128, 2 * D], BF16))
            bn6b = eb(nc.sbuf_tensor("bn6b", [128, 12], F32))
            mvb = eb(nc.sbuf_tensor("mvb", [128, 2], F32))
            lnvb = eb(nc.sbuf_tensor("lnvb", [128, 1], F32))
            rstdb = eb(nc.sbuf_tensor("rstdb", [128, 1], F32))
            nmax = eb(nc.sbuf_tensor("nmax", [128, 1], F32))
            lsum = eb(nc.sbuf_tensor("lsum", [128, 1], F32))
            rl = eb(nc.sbuf_tensor("rl", [128, 1], F32))
            pTb = eb(nc.psum_tensor("pTb", [128, 1024], BF16))
            pTb2 = eb(nc.psum_tensor("pTb2", [128, 1024], BF16))
            pS = eb(nc.psum_tensor("pS", [128, 1024], F32))
            pPV = eb(nc.psum_tensor("pPV", [128, 1024], F32))
            pLn = eb(nc.psum_tensor("pLn", [128, 1024], BF16))

            def emit_B(en, g, rec):
                def W(sm, key):
                    if not rec:
                        g.wait_ge(SEM[sm], toks[key])

                def WV(sm, val):
                    if not rec:
                        g.wait_ge(SEM[sm], val)

                if not rec:
                    g.wait_ge(bar, 5)

                if en == "sync":
                    w = Prod(bw, rec, toks)
                    for jt in range(NT):
                        if jt >= 2:
                            W("bv", ("Bres", jt - 2))
                        w.inc(g.dma_start(
                            out=xj[:, (jt % 2) * D:(jt % 2 + 1) * D],
                            in_=x_hbm[jt * 128:(jt + 1) * 128, :]),
                            ("Bx", jt), 16)
                    if not rec:
                        g.sem_inc(bar, 1)
                    return

                if en == "vector":
                    v = Prod(bv, rec, toks)

                    def emit_n1(jt):
                        W("bw", ("Bx", jt))
                        v.inc(g.tensor_scalar(
                            out=ln1j[:], in0=xj[:, (jt % 2) * D:(jt % 2 + 1) * D],
                            scalar1=stats[:, 2 * jt:2 * jt + 1],
                            scalar2=stats[:, 2 * jt + 1:2 * jt + 2],
                            op0=ALU.subtract, op1=ALU.mult), ("Bn1", jt))

                    def emit_lnT_evac(jt):
                        W("bt", ("Btp", jt))
                        ins = None
                        for c in range(8):
                            ins = g.tensor_copy(
                                out=lnTj[:, c * 128:(c + 1) * 128],
                                in_=pLn[:, c * 128:(c + 1) * 128])
                        v.inc(ins, ("BlnT", jt))

                    emit_n1(0)
                    emit_lnT_evac(0)
                    for jt in range(NT):
                        for ic in range(8):
                            W("bt", ("Bts", jt, ic))
                            v.inc(g.tensor_copy(
                                out=S_sb[:, ic * 512:(ic + 1) * 512],
                                in_=pS[:, (ic % 2) * 512:(ic % 2 + 1) * 512]),
                                ("Bse", jt, ic))
                        v.inc(g.reduce_max(nmax[:], S_sb[:], axis=AX.X,
                                           negate=True), ("Bmx", jt))
                        if jt + 1 < NT:
                            emit_n1(jt + 1)
                            emit_lnT_evac(jt + 1)
                        W("bs", ("Bex", jt))
                        v.inc(g.reciprocal(rl[:], lsum[:]), None)
                        for i in range(NT):
                            W("bt", ("Btpt", jt, (i // 8) * 8 + 7))
                            ptb = pTb if (i // 8) % 2 == 0 else pTb2
                            v.inc(g.tensor_copy(
                                out=PT_sb[:, i * 128:(i + 1) * 128],
                                in_=ptb[:, (i % 8) * 128:(i % 8 + 1) * 128]),
                                ("Bpte", jt, i))
                        W("bt", ("Bpv", jt))
                        g.tensor_scalar(out=attn[:, 0:512], in0=pPV[:, 0:512],
                                        scalar1=rl[:], scalar2=None, op0=ALU.mult)
                        v.inc(g.tensor_scalar(
                            out=attn[:, 512:1024], in0=pPV[:, 512:1024],
                            scalar1=rl[:], scalar2=None, op0=ALU.mult),
                            ("Bat", jt))
                        g.bn_stats(bn6b[:, 0:6], attn[:, 0:512])
                        i0 = v.inc(g.bn_stats(bn6b[:, 6:12], attn[:, 512:1024]),
                                   None)
                        WV("bv", i0)
                        v.inc(g.bn_aggr(mvb[:], bn6b[:]), ("Bag", jt))
                        W("bs", ("Brs", jt))
                        v.inc(g.tensor_scalar(
                            out=tres[:], in0=attn[:], scalar1=mvb[:, 0:1],
                            scalar2=rstdb[:], op0=ALU.subtract, op1=ALU.mult),
                            None)
                        g.tensor_add(attn[:], tres[:],
                                     xj[:, (jt % 2) * D:(jt % 2 + 1) * D])
                        if jt >= 2:
                            WV("bg", 16 * (jt - 1))
                        v.inc(g.tensor_copy(
                            out=x2j[:, (jt % 2) * D:(jt % 2 + 1) * D],
                            in_=attn[:]), ("Bres", jt))
                    if not rec:
                        g.sem_inc(bar, 1)
                    return

                if en == "tensor":
                    t = Prod(bt, rec, toks)

                    def emit_tp(jt):
                        W("bv", ("Bn1", jt))
                        if jt > 0:
                            W("bv", ("BlnT", jt - 1))
                        ins = None
                        for c in range(8):
                            ins = g.transpose(
                                pLn[:, c * 128:(c + 1) * 128],
                                ln1j[:, c * 128:(c + 1) * 128], idb[:])
                        t.inc(ins, ("Btp", jt))

                    def emit_kt(jt):
                        W("bv", ("BlnT", jt))
                        for ck in range(8):
                            if jt > 0:
                                W("bv", ("Bse", jt - 1, 6 + ck % 2))
                            if ck >= 2:
                                W("bs", ("Bke", jt, ck - 2))
                            elif jt > 0:
                                W("bs", ("Bke", jt - 1, ck + 6))
                            ins = None
                            for c in range(8):
                                ins = g.matmul(
                                    pS[:, (ck % 2) * 512:(ck % 2) * 512 + 128],
                                    wk_sb[:, c * D + ck * 128:
                                          c * D + (ck + 1) * 128],
                                    lnTj[:, c * 128:(c + 1) * 128],
                                    start=(c == 0), stop=(c == 7))
                            t.inc(ins, ("Btk", jt, ck))

                    emit_tp(0)
                    emit_kt(0)
                    for jt in range(NT):
                        W("bs", ("Bke", jt, 7))
                        for ic in range(8):
                            if ic >= 2:
                                W("bv", ("Bse", jt, ic - 2))
                            elif jt > 0:
                                W("bv", ("Bse", jt - 1, ic + 6))
                            ins = None
                            for c in range(8):
                                ins = g.matmul(
                                    pS[:, (ic % 2) * 512:(ic % 2 + 1) * 512],
                                    kTj[:, c * 128:(c + 1) * 128],
                                    U1[:, c * T + ic * 512:c * T + (ic + 1) * 512],
                                    start=(c == 0), stop=(c == 7))
                            t.inc(ins, ("Bts", jt, ic))
                        if jt + 1 < NT:
                            emit_tp(jt + 1)
                            emit_kt(jt + 1)
                        W("bs", ("Bex", jt))
                        if jt + 1 >= NT:
                            W("bv", ("BlnT", jt))
                        for i in range(NT):
                            if i == 0 and jt > 0:
                                W("bv", ("Bpte", jt - 1, NT - 1))
                            if i % 8 == 0 and i >= 16:
                                W("bv", ("Bpte", jt, i - 9))
                            ptb = pTb if (i // 8) % 2 == 0 else pTb2
                            t.inc(g.transpose(
                                ptb[:, (i % 8) * 128:(i % 8 + 1) * 128],
                                P_bf[:, i * 128:(i + 1) * 128], idb[:]),
                                ("Btpt", jt, i))
                        if jt > 0:
                            W("bv", ("Bat", jt - 1))
                        ins = None
                        for i in range(NT):
                            W("bv", ("Bpte", jt, i))
                            for n in range(2):
                                ins = g.matmul(
                                    pPV[:, n * 512:(n + 1) * 512],
                                    PT_sb[:, i * 128:(i + 1) * 128],
                                    U2[:, i * D + n * 512:i * D + (n + 1) * 512],
                                    start=(i == 0), stop=(i == NT - 1))
                        t.inc(ins, ("Bpv", jt))
                    if not rec:
                        g.sem_inc(bar, 1)
                    return

                if en == "scalar":
                    s = Prod(bs, rec, toks)

                    def emit_kev(jt):
                        for ck in range(8):
                            W("bt", ("Btk", jt, ck))
                            s.inc(g.activation(
                                out=kTj[:, ck * 128:(ck + 1) * 128],
                                in_=pS[:, (ck % 2) * 512:(ck % 2) * 512 + 128],
                                func=AF.Copy), ("Bke", jt, ck))

                    emit_kev(0)
                    for jt in range(NT):
                        W("bv", ("Bmx", jt))
                        s.inc(g.activation(
                            out=P_bf[:], in_=S_sb[:], func=AF.Exp,
                            bias=nmax[:], accum_out=lsum[:]), ("Bex", jt))
                        if jt + 1 < NT:
                            emit_kev(jt + 1)
                        W("bv", ("Bag", jt))
                        i0 = s.inc(g.activation(out=lnvb[:], in_=mvb[:, 1:2],
                                                func=AF.Ln, bias=seps[:]), None)
                        WV("bs", i0)
                        s.inc(g.activation(out=rstdb[:], in_=lnvb[:],
                                           func=AF.Exp, scale=-0.5), ("Brs", jt))
                    if not rec:
                        g.sem_inc(bar, 1)
                    return

                if en == "gpsimd":
                    gp = Prod(bg, rec, toks)
                    for jt in range(NT):
                        W("bv", ("Bres", jt))
                        gp.inc(g.dma_start(
                            out=x2s[jt * 128:(jt + 1) * 128, :],
                            in_=x2j[:, (jt % 2) * D:(jt % 2 + 1) * D]),
                            ("Bx2", jt), 16)
                    if not rec:
                        g.sem_inc(bar, 1)
                    return

            run_phase(emit_B)

        abctx.close()

        NSUB = 2
        TW = 256
        with ExitStack() as cctx:
            ec = cctx.enter_context
            ones1 = ec(nc.sbuf_tensor("ones1", [1, 128], F32))
            onesr = ec(nc.sbuf_tensor("onesr", [1, TW], BF16))
            bo2bf = ec(nc.sbuf_tensor("bo2bf", [1, D], BF16))
            gbb = ec(nc.sbuf_tensor("gbb", [128, 8], F32))
            io8 = ec(nc.sbuf_tensor("io8_s", [128, 8], F32))
            ioPB = ec(nc.sbuf_tensor("ioPB_s", [128, 8], F32))
            ediv = ec(nc.sbuf_tensor("ediv_s", [128, 1], F32))
            biP = ec(nc.sbuf_tensor("biP_s", [128, 32], F32))
            gwT = ec(nc.sbuf_tensor("gwT_s", [128, 8 * 8], BF16))
            aT = ec(nc.sbuf_tensor("aT_s", [128, 8 * 128], BF16))
            bT = ec(nc.sbuf_tensor("bT_s", [128, DF], BF16))
            cst = ec(nc.sbuf_tensor("cst", [128, 2 * D], F32))
            x2st = ec(nc.sbuf_tensor("x2st", [128, 2 * D], BF16))
            x2re = ec(nc.sbuf_tensor("x2re", [128, 2 * D], BF16))
            ln2 = ec(nc.sbuf_tensor("ln2", [128, D], BF16))
            lnT2 = ec(nc.sbuf_tensor("lnT2", [128, 2 * 8 * TW], BF16))
            lg = ec(nc.sbuf_tensor("lg", [128, 8], F32))
            pe_ = ec(nc.sbuf_tensor("pe_", [128, 8], F32))
            prob = ec(nc.sbuf_tensor("prob", [128, 8], F32))
            p2 = ec(nc.sbuf_tensor("p2", [128, 8], F32))
            eq = ec(nc.sbuf_tensor("eq", [128, 8], F32))
            tt8 = ec(nc.sbuf_tensor("tt8", [128, 8], F32))
            excl = ec(nc.sbuf_tensor("excl", [128, 8], F32))
            v1c = ec(nc.sbuf_tensor("v1c", [128, 1], F32))
            v2c = ec(nc.sbuf_tensor("v2c", [128, 1], F32))
            idx1 = ec(nc.sbuf_tensor("idx1", [128, 1], F32))
            idx2 = ec(nc.sbuf_tensor("idx2", [128, 1], F32))
            vs4 = ec(nc.sbuf_tensor("vs4", [128, 2 * NSUB], F32))
            nmg = ec(nc.sbuf_tensor("nmg", [128, 1], F32))
            sme = ec(nc.sbuf_tensor("sme", [128, 1], F32))
            rg = ec(nc.sbuf_tensor("rg", [128, 1], F32))
            id1T = ec(nc.sbuf_tensor("id1T", [1, TW], BF16))
            id2T = ec(nc.sbuf_tensor("id2T", [1, TW], BF16))
            tmpb = ec(nc.sbuf_tensor("tmpb", [128, TW], BF16))
            tm0 = ec(nc.sbuf_tensor("tm0", [128, TW], BF16))
            tm1 = ec(nc.sbuf_tensor("tm1", [128, TW], BF16))
            bcf = ec(nc.sbuf_tensor("bcf", [128, 2 * TW], BF16))
            hsum = ec(nc.sbuf_tensor("hsum", [128, 32 * TW], BF16))
            wotm = ec(nc.sbuf_tensor("wotm", [128, 2 * TW], F32))
            ffn = ec(nc.sbuf_tensor("ffn", [128, NSUB * D], BF16))
            outb = ec(nc.sbuf_tensor("outb", [128, 2 * D], F32))
            bn6c = ec(nc.sbuf_tensor("bn6c", [128, 12], F32))
            mvc = ec(nc.sbuf_tensor("mvc", [128, 2], F32))
            lnvc = ec(nc.sbuf_tensor("lnvc", [128, 1], F32))
            rstdc = ec(nc.sbuf_tensor("rstdc", [128, 1], F32))
            t3b = ec(nc.sbuf_tensor("t3b", [128, D], BF16))
            y2 = cst[:, 0:D]
            t3 = t3b[:]
            pTc = ec(nc.psum_tensor("pTc", [128, 1024], BF16))
            pTf = ec(nc.psum_tensor("pTf", [128, 512], F32))
            pPool = ec(nc.psum_tensor("pPool", [128, 1024], F32))
            pPool2 = ec(nc.psum_tensor("pPool2", [128, 1024], F32))
            pO = ec(nc.psum_tensor("pO", [128, 1024], F32))
            pG = pTf[:, 256:264]

            def pool(i):
                tn, sl = divmod(i % 4, 2)
                ps = pPool if tn == 0 else pPool2
                return ps[:, sl * 512:sl * 512 + TW]

            units = []
            for c in range(8):
                units.append((gwT[:, c * 8:(c + 1) * 8],
                              gwT_h[c * 128:(c + 1) * 128, :], 8))
            for c in range(8):
                units.append((aT[:, c * 128:(c + 1) * 128],
                              aT_h[c * 128:(c + 1) * 128, :], 128))
            for c in range(4):
                units.append((bT[:, c * 1024:(c + 1) * 1024],
                              bT_h[:, c * 1024:(c + 1) * 1024], 1024))
            for k in range(32):
                c, h = divmod(k, 4)
                units.append((U1[:, c * T + h * 1024:c * T + (h + 1) * 1024],
                              wiT_h[c * 128:(c + 1) * 128,
                                    h * 1024:(h + 1) * 1024], 1024))
            for cf in range(32):
                units.append((U2[:, cf * D:(cf + 1) * D],
                              woT_h[cf * 128:(cf + 1) * 128, :], 1024))
            NW = len(units)
            ccst = [(biP[:], biP_h), (io8[:], io8_h), (ioPB[:], ioPB_h),
                    (ediv[:], ediv_h), (gbb[0:1, :], gb_h),
                    (cst[0:1, 0:D], bo_h)]

            def emit_C(en, g, rec):
                def W(sm, key):
                    if not rec:
                        g.wait_ge(SEM[sm], toks[key])

                def WV(sm, val):
                    if not rec:
                        g.wait_ge(SEM[sm], val)

                if not rec:
                    g.wait_ge(bar, 10)

                def lnT2s(t, c, lo, hi):
                    gn = t % 2
                    return lnT2[:, gn * 2048 + c * TW + lo:gn * 2048 + c * TW + hi]

                if en == "sync":
                    w = Prod(cw, rec, toks)
                    for i, (dst, srcap) in enumerate(ccst):
                        w.inc(g.dma_start(out=dst, in_=srcap), ("Cc", i), 16)
                    for k in range(NW):
                        if k == 0:
                            W("cv", ("Cbo2",))
                        if k >= 2:
                            W("cv", ("Cwc", k - 2))
                        dst, srcap, ncol = units[k]
                        w.inc(g.dma_start(
                            out=cst[:, (k % 2) * D:(k % 2) * D + ncol],
                            in_=srcap), ("Cwd", k), 16)
                    for u in range(NT):
                        W("cv", ("Cout", u))
                        w.inc(g.dma_start(
                            out=out_hbm[u * 128:(u + 1) * 128, :],
                            in_=outb[:, (u % 2) * D:(u % 2 + 1) * D]),
                            ("Cst", u), 16)
                    if not rec:
                        g.wait_ge(cw, w.n)
                    return

                if en == "vector":
                    v = Prod(cv, rec, toks)

                    def ln2_block(u):
                        W("cg", ("Cx2", u))
                        xs = x2st[:, (u % 2) * D:(u % 2 + 1) * D]
                        g.bn_stats(bn6c[:, 0:6], xs[:, 0:512])
                        i0 = v.inc(g.bn_stats(bn6c[:, 6:12], xs[:, 512:1024]),
                                   None)
                        WV("cv", i0)
                        v.inc(g.bn_aggr(mvc[:], bn6c[:]), ("Cag", u))
                        W("cs", ("Crs", u))
                        v.inc(g.tensor_scalar(
                            out=ln2[:], in0=xs, scalar1=mvc[:, 0:1],
                            scalar2=rstdc[:], op0=ALU.subtract, op1=ALU.mult),
                            ("Cn2", u))

                    def lnT2_evac(u):
                        t, sl = divmod(u, NSUB)
                        W("ct", ("Ctp2", u))
                        ins = None
                        for c in range(8):
                            ins = g.tensor_copy(
                                out=lnT2s(t, c, sl * 128, (sl + 1) * 128),
                                in_=pTc[:, c * 128:(c + 1) * 128])
                        v.inc(ins, ("ClT", u))

                    def gate_a(u):
                        W("ct", ("Cgm", u))
                        if u == 0:
                            W("cw", ("Cc", 2))
                        i0 = v.inc(g.tensor_add(lg[:], pG, gbb[:]), None)
                        WV("cv", i0)
                        v.inc(g.reduce_max(nmg[:], lg[:], axis=AX.X,
                                           negate=True), ("Cgx", u))

                    def gate_b(u):
                        t, sl = divmod(u, NSUB)
                        W("cs", ("Cge", u))
                        i0 = v.inc(g.reciprocal(rg[:], sme[:]), None)
                        WV("cv", i0)
                        i0 = v.inc(g.tensor_scalar(
                            out=prob[:], in0=pe_[:], scalar1=rg[:],
                            scalar2=None, op0=ALU.mult), None)
                        WV("cv", i0)
                        i0 = v.inc(g.reduce_max(v1c[:], prob[:], axis=AX.X),
                                   None)
                        WV("cv", i0)
                        i0 = v.inc(g.tensor_scalar(
                            out=eq[:], in0=prob[:], scalar1=v1c[:],
                            scalar2=None, op0=ALU.is_equal), None)
                        WV("cv", i0)
                        i0 = v.inc(g.scalar_tensor_tensor(
                            out=tt8[:], in0=eq[:], scalar=-1024.0,
                            in1=ioPB[:], op0=ALU.mult, op1=ALU.add), None)
                        WV("cv", i0)
                        i0 = v.inc(g.tensor_reduce(
                            idx1[:], tt8[:], axis=AX.X, op=ALU.min), None)
                        WV("cv", i0)
                        i0 = v.inc(g.tensor_scalar(
                            out=excl[:], in0=io8[:], scalar1=idx1[:],
                            scalar2=None, op0=ALU.is_equal), None)
                        WV("cv", i0)
                        i0 = v.inc(g.scalar_tensor_tensor(
                            out=p2[:], in0=excl[:], scalar=-2.0,
                            in1=prob[:], op0=ALU.mult, op1=ALU.add), None)
                        WV("cv", i0)
                        i0 = v.inc(g.reduce_max(v2c[:], p2[:], axis=AX.X),
                                   None)
                        WV("cv", i0)
                        i0 = v.inc(g.tensor_scalar(
                            out=eq[:], in0=p2[:], scalar1=v2c[:],
                            scalar2=None, op0=ALU.is_equal), None)
                        WV("cv", i0)
                        i0 = v.inc(g.scalar_tensor_tensor(
                            out=tt8[:], in0=eq[:], scalar=-1024.0,
                            in1=ioPB[:], op0=ALU.mult, op1=ALU.add), None)
                        WV("cv", i0)
                        i0 = v.inc(g.tensor_reduce(
                            idx2[:], tt8[:], axis=AX.X, op=ALU.min), None)
                        WV("cv", i0)
                        v.inc(g.tensor_add(vs4[:, (t % 2) * 2 + sl:
                                               (t % 2) * 2 + sl + 1],
                                           v1c[:], v2c[:]), ("Cgt", u))

                    def idx_evac(u):
                        sl = u % NSUB
                        W("ct", ("CiT", u))
                        g.tensor_copy(out=id1T[:, sl * 128:(sl + 1) * 128],
                                      in_=pTf[0:1, 0:128])
                        v.inc(g.tensor_copy(
                            out=id2T[:, sl * 128:(sl + 1) * 128],
                            in_=pTf[0:1, 128:256]), ("Cie", u))

                    def tmp_masks(t):
                        W("ct", ("Ctm", t))
                        i0 = v.inc(g.tensor_copy(out=tmpb[:],
                                                 in_=pPool[:, 0:TW]), None)
                        WV("cv", i0)
                        W("ct", ("Cir", t))
                        if t == 0:
                            W("cw", ("Cc", 3))
                        g.scalar_tensor_tensor(
                            out=tm0[:], in0=pPool[:, 512:512 + TW],
                            scalar=ediv[:], in1=tmpb[:],
                            op0=ALU.is_equal, op1=ALU.mult)
                        v.inc(g.scalar_tensor_tensor(
                            out=tm1[:], in0=pPool2[:, 0:TW], scalar=ediv[:],
                            in1=tmpb[:], op0=ALU.is_equal, op1=ALU.mult),
                            ("Cmk", t))

                    g.memset(ones1[:], 1.0)
                    v.inc(g.memset(onesr[:], 1.0), ("Cms",))
                    W("cw", ("Cc", 5))
                    v.inc(g.tensor_scalar(
                        out=bo2bf[:], in0=cst[0:1, 0:D], scalar1=2.0,
                        scalar2=None, op0=ALU.mult), ("Cbo2",))
                    for k in range(NW):
                        W("cw", ("Cwd", k))
                        dst, srcap, ncol = units[k]
                        v.inc(g.tensor_copy(
                            out=dst, in_=cst[:, (k % 2) * D:(k % 2) * D + ncol]),
                            ("Cwc", k))
                    for u in range(NSUB):
                        ln2_block(u)
                        lnT2_evac(u)
                        gate_a(u)
                        gate_b(u)
                        idx_evac(u)
                    tmp_masks(0)
                    for t in range(NTC):
                        nxt = t + 1 < NTC
                        u0, u1 = (t + 1) * NSUB, (t + 1) * NSUB + 1
                        for cf in range(32):
                            if nxt:
                                if cf == 1:
                                    ln2_block(u0)
                                elif cf == 5:
                                    lnT2_evac(u0)
                                elif cf == 7:
                                    gate_a(u0)
                                elif cf == 9:
                                    gate_b(u0)
                                elif cf == 12:
                                    idx_evac(u0)
                                elif cf == 15:
                                    ln2_block(u1)
                                elif cf == 19:
                                    lnT2_evac(u1)
                                elif cf == 21:
                                    gate_a(u1)
                                elif cf == 23:
                                    gate_b(u1)
                                elif cf == 26:
                                    idx_evac(u1)
                            W("ct", ("Cs1", t, cf))
                            W("cs", ("Ch0", t, cf))
                            v.inc(g.scalar_tensor_tensor(
                                out=hsum[:, cf * TW:(cf + 1) * TW],
                                in0=pool(3 * cf + 2), scalar=0.0,
                                in1=hsum[:, cf * TW:(cf + 1) * TW],
                                op0=ALU.max, op1=ALU.add), ("Ch1", t, cf))
                        if nxt:
                            tmp_masks(t + 1)
                        for cd in range(8):
                            for m in range(NSUB):
                                W("ct", ("Cwt", t, cd, NSUB - 1))
                                v.inc(g.tensor_copy(
                                    out=ffn[:, m * D + cd * 128:
                                            m * D + (cd + 1) * 128],
                                    in_=pTf[:, m * 128:(m + 1) * 128]),
                                    ("Cfa", t, cd, m))
                        for m in range(NSUB):
                            u = t * NSUB + m
                            i0 = v.inc(g.tensor_scalar(
                                out=y2, in0=ffn[:, m * D:(m + 1) * D],
                                scalar1=vs4[:, (t % 2) * 2 + m:
                                            (t % 2) * 2 + m + 1],
                                scalar2=None, op0=ALU.mult), None)
                            WV("cv", i0)
                            g.bn_stats(bn6c[:, 0:6], y2[:, 0:512])
                            i0 = v.inc(g.bn_stats(bn6c[:, 6:12], y2[:, 512:1024]),
                                       None)
                            WV("cv", i0)
                            v.inc(g.bn_aggr(mvc[:], bn6c[:]), ("Ca3", u))
                            W("cs", ("Cr3", u))
                            i0 = v.inc(g.tensor_scalar(
                                out=t3, in0=y2, scalar1=mvc[:, 0:1],
                                scalar2=rstdc[:], op0=ALU.subtract, op1=ALU.mult),
                                None)
                            WV("cv", i0)
                            W("cx", ("Cxr", u))
                            if u >= 2:
                                W("cw", ("Cst", u - 2))
                            v.inc(g.tensor_add(
                                outb[:, (u % 2) * D:(u % 2 + 1) * D], t3,
                                x2re[:, (u % 2) * D:(u % 2 + 1) * D]),
                                ("Cout", u))
                    return

                if en == "tensor":
                    t_ = Prod(ct, rec, toks)

                    def tp2(u):
                        t, sl = divmod(u, NSUB)
                        W("cv", ("Cn2", u))
                        if u > 0:
                            W("cv", ("ClT", u - 1))
                        ins = None
                        for c in range(8):
                            ins = g.transpose(
                                pTc[:, c * 128:(c + 1) * 128],
                                ln2[:, c * 128:(c + 1) * 128], idb[:])
                        t_.inc(ins, ("Ctp2", u))

                    def gmm(u):
                        t, sl = divmod(u, NSUB)
                        W("cv", ("ClT", u))
                        if u == 0:
                            W("cv", ("Cwc", 7))
                        if u > 0:
                            W("cv", ("Cie", u - 1))
                        if sl == 0 and t >= 2:
                            W("cv", ("Cfa", t - 2, 7, NSUB - 1))
                        ins = None
                        for c in range(8):
                            ins = g.matmul(
                                pG, lnT2s(t, c, sl * 128, (sl + 1) * 128),
                                gwT[:, c * 8:(c + 1) * 8],
                                start=(c == 0), stop=(c == 7))
                        t_.inc(ins, ("Cgm", u))

                    def idxT(u):
                        W("cv", ("Cgt", u))
                        g.transpose(pTf[0:1, 0:128], idx1[:], idf[:])
                        t_.inc(g.transpose(pTf[0:1, 128:256], idx2[:],
                                           idf[:]), ("CiT", u))

                    def tmp_irep(t):
                        if t > 0:
                            W("cv", ("Ch1", t - 1, 30))
                            W("cs", ("Ch0", t - 1, 31))
                        else:
                            W("cv", ("Cwc", 19))
                        W("cv", ("ClT", t * NSUB + 1))
                        ins = None
                        for c in range(8):
                            ins = g.matmul(
                                pPool[:, 0:TW], aT[:, c * 128:(c + 1) * 128],
                                lnT2s(t, c, 0, TW),
                                start=(c == 0), stop=(c == 7))
                        t_.inc(ins, ("Ctm", t))
                        W("cv", ("Cie", t * NSUB + 1))
                        g.matmul(pPool[:, 512:512 + TW], onesr[0:1, 0:128],
                                 id1T[:], start=True, stop=True)
                        t_.inc(g.matmul(pPool2[:, 0:TW], onesr[0:1, 0:128],
                                        id2T[:], start=True, stop=True),
                               ("Cir", t))

                    W("cv", ("Cms",))
                    W("cw", ("Cc", 4))
                    t_.inc(g.matmul(pG, ones1[:], gbb[0:1, :],
                                    start=True, stop=True), ("Cgb",))
                    W("cs", ("Cbc",))
                    for u in range(NSUB):
                        tp2(u)
                        gmm(u)
                        idxT(u)
                    tmp_irep(0)
                    for t in range(NTC):
                        nxt = t + 1 < NTC
                        u0, u1 = (t + 1) * NSUB, (t + 1) * NSUB + 1
                        W("cv", ("Cmk", t))
                        if t == 0:
                            W("cv", ("Cwc", 51))
                        for cf in range(32):
                            if nxt:
                                if cf == 4:
                                    tp2(u0)
                                elif cf == 6:
                                    gmm(u0)
                                elif cf == 11:
                                    idxT(u0)
                                elif cf == 18:
                                    tp2(u1)
                                elif cf == 20:
                                    gmm(u1)
                                elif cf == 25:
                                    idxT(u1)
                            if cf >= 2:
                                W("cv", ("Ch1", t, cf - 2))
                            ins = None
                            for c in range(8):
                                ins = g.matmul(
                                    pool(3 * cf),
                                    U1[:, c * T + cf * 128:c * T + (cf + 1) * 128],
                                    lnT2s(t, c, 0, TW),
                                    start=(c == 0), stop=(c == 7))
                            t_.inc(ins, ("Cba", t, cf))
                            W("cs", ("Cbe", t, cf))
                            g.matmul(pool(3 * cf + 1),
                                     bT[:, cf * 128:(cf + 1) * 128], tm0[:],
                                     start=True, stop=False)
                            t_.inc(g.matmul(
                                pool(3 * cf + 1), idb[:],
                                bcf[:, (cf % 2) * TW:(cf % 2 + 1) * TW],
                                start=False, stop=True), ("Cs0", t, cf))
                            g.matmul(pool(3 * cf + 2),
                                     bT[:, cf * 128:(cf + 1) * 128], tm1[:],
                                     start=True, stop=False)
                            t_.inc(g.matmul(
                                pool(3 * cf + 2), idb[:],
                                bcf[:, (cf % 2) * TW:(cf % 2 + 1) * TW],
                                start=False, stop=True), ("Cs1", t, cf))
                        if nxt:
                            tmp_irep(t + 1)
                        if t == 0:
                            W("cv", ("Cwc", 83))
                        for cd in range(8):
                            W("cv", ("Ch1", t, 31))
                            if cd >= 2:
                                W("cs", ("Coe", t, cd - 2))
                            for cf in range(32):
                                g.matmul(
                                    pO[:, (cd % 2) * 512:(cd % 2) * 512 + TW],
                                    U2[:, cf * D + cd * 128:cf * D + (cd + 1) * 128],
                                    hsum[:, cf * TW:(cf + 1) * TW],
                                    start=(cf == 0), stop=False)
                            t_.inc(g.matmul(
                                pO[:, (cd % 2) * 512:(cd % 2) * 512 + TW],
                                bo2bf[0:1, cd * 128:(cd + 1) * 128],
                                onesr[:], start=False, stop=True),
                                ("Cwo", t, cd))
                            W("cs", ("Coe", t, cd))
                            if cd > 0:
                                W("cv", ("Cfa", t, cd - 1, NSUB - 1))
                            elif nxt:
                                W("cv", ("Cie", u1))
                            for m in range(NSUB):
                                t_.inc(g.transpose(
                                    pTf[:, m * 128:(m + 1) * 128],
                                    wotm[:, (cd % 2) * TW + m * 128:
                                         (cd % 2) * TW + (m + 1) * 128],
                                    idf[:]), ("Cwt", t, cd, m))
                    return

                if en == "scalar":
                    s = Prod(cs, rec, toks)
                    sx = Prod(cx, rec, toks)

                    def x2re_dma(u):
                        sx.inc(g.dma_start(
                            out=x2re[:, (u % 2) * D:(u % 2 + 1) * D],
                            in_=x2s[u * 128:(u + 1) * 128, :]), ("Cxr", u), 16)

                    def rstd_chain(u):
                        W("cv", ("Cag", u))
                        i0 = s.inc(g.activation(out=lnvc[:], in_=mvc[:, 1:2],
                                                func=AF.Ln, bias=seps[:]), None)
                        WV("cs", i0)
                        s.inc(g.activation(out=rstdc[:], in_=lnvc[:],
                                           func=AF.Exp, scale=-0.5), ("Crs", u))

                    def gexp(u):
                        W("cv", ("Cgx", u))
                        s.inc(g.activation(out=pe_[:], in_=lg[:], func=AF.Exp,
                                           bias=nmg[:], accum_out=sme[:]),
                              ("Cge", u))

                    W("ct", ("Cgb",))
                    s.inc(g.activation(out=gbb[:], in_=pG, func=AF.Copy),
                          ("Cbc",))
                    x2re_dma(0)
                    x2re_dma(1)
                    for u in range(NSUB):
                        rstd_chain(u)
                        gexp(u)
                    for t in range(NTC):
                        nxt = t + 1 < NTC
                        u0, u1 = (t + 1) * NSUB, (t + 1) * NSUB + 1
                        for cf in range(32):
                            if nxt:
                                if cf == 2:
                                    rstd_chain(u0)
                                elif cf == 8:
                                    gexp(u0)
                                elif cf == 16:
                                    rstd_chain(u1)
                                elif cf == 22:
                                    gexp(u1)
                            W("ct", ("Cba", t, cf))
                            s.inc(g.activation(
                                out=bcf[:, (cf % 2) * TW:(cf % 2 + 1) * TW],
                                in_=pool(3 * cf), func=AF.Identity,
                                bias=biP[:, cf:cf + 1]), ("Cbe", t, cf))
                            W("ct", ("Cs0", t, cf))
                            s.inc(g.activation(
                                out=hsum[:, cf * TW:(cf + 1) * TW],
                                in_=pool(3 * cf + 1), func=AF.Relu),
                                ("Ch0", t, cf))
                        for cd in range(8):
                            W("ct", ("Cwo", t, cd))
                            if cd >= 2:
                                W("ct", ("Cwt", t, cd - 2, NSUB - 1))
                            s.inc(g.activation(
                                out=wotm[:, (cd % 2) * TW:(cd % 2 + 1) * TW],
                                in_=pO[:, (cd % 2) * 512:(cd % 2) * 512 + TW],
                                func=AF.Copy), ("Coe", t, cd))
                        for m in range(NSUB):
                            u = t * NSUB + m
                            W("cv", ("Ca3", u))
                            i0 = s.inc(g.activation(out=lnvc[:], in_=mvc[:, 1:2],
                                                    func=AF.Ln, bias=seps[:]),
                                       None)
                            WV("cs", i0)
                            s.inc(g.activation(out=rstdc[:], in_=lnvc[:],
                                               func=AF.Exp, scale=-0.5),
                                  ("Cr3", u))
                        for m in range(NSUB):
                            u = t * NSUB + m
                            if u + 2 < NT:
                                W("cv", ("Cout", u))
                                x2re_dma(u + 2)
                    return

                if en == "gpsimd":
                    gp = Prod(cg, rec, toks)
                    for u in range(NT):
                        if u >= 2:
                            W("cv", ("Cn2", u - 2))
                        gp.inc(g.dma_start(
                            out=x2st[:, (u % 2) * D:(u % 2 + 1) * D],
                            in_=x2s[u * 128:(u + 1) * 128, :]), ("Cx2", u), 16)
                    if not rec:
                        g.wait_ge(cg, gp.n)
                    return

            run_phase(emit_C)

    return nc


def kernel(**inputs):
    f32 = np.float32
    x = np.asarray(inputs["L0_attn_x"], dtype=f32)
    wq = np.ascontiguousarray(np.asarray(inputs["L0_attn_W_query"], f32))
    wk = np.ascontiguousarray(np.asarray(inputs["L0_attn_W_key"], f32))
    wv = np.ascontiguousarray(np.asarray(inputs["L0_attn_W_value"], f32))
    wiT = np.ascontiguousarray(np.asarray(inputs["L0_ffn_wi"], f32).T)
    woT = np.ascontiguousarray(np.asarray(inputs["L0_ffn_wo"], f32).T)
    biP = np.ascontiguousarray(
        np.asarray(inputs["L0_ffn_bi"], f32).reshape(32, 128).T)
    bo = np.ascontiguousarray(np.asarray(inputs["L0_ffn_bo"], f32).reshape(1, D))
    aT = np.ascontiguousarray(
        np.asarray(inputs["L0_ffn_lora_A"], f32).transpose(2, 0, 1).reshape(D, 128))
    bT = np.ascontiguousarray(
        np.asarray(inputs["L0_ffn_lora_B"], f32).transpose(0, 2, 1).reshape(128, DF))
    gwT = np.ascontiguousarray(np.asarray(inputs["L0_ffn_gate_w"], f32).T)
    gb = np.ascontiguousarray(np.asarray(inputs["L0_ffn_gate_b"], f32).reshape(1, 8))
    idf = np.eye(128, dtype=f32)
    io8 = np.ascontiguousarray(np.tile(np.arange(8, dtype=f32), (128, 1)))
    ioPB = io8 + 1024.0
    ediv = (np.arange(128) // 16).astype(f32).reshape(128, 1)

    shared = dict(wq=wq, wk=wk, wv=wv, wiT=wiT, woT=woT, biP=biP, bo=bo,
                  aT=aT, bT=bT, gwT=gwT, gb=gb, idf=idf, io8=io8,
                  ioPB=ioPB, ediv=ediv)
    in_maps = [dict(shared, x=np.ascontiguousarray(x[b])) for b in range(8)]

    nc = build_and_emit()
    res = run_bass_kernel_spmd(nc, in_maps, core_ids=list(range(8)))
    return np.stack([res.results[b]["out"] for b in range(8)], axis=0)
